# version 01 (final)
# baseline (speedup 1.0000x reference)
import numpy as np
import ml_dtypes
import concourse.bass as bass
import concourse.mybir as mybir
from concourse.bass_utils import run_bass_kernel_spmd
from contextlib import ExitStack

F32 = mybir.dt.float32
BF16 = mybir.dt.bfloat16
I8 = mybir.dt.int8
AF = mybir.ActivationFunctionType
ALU = mybir.AluOpType
AX = mybir.AxisListType
BF = ml_dtypes.bfloat16

ENGS = ["sync", "scalar", "vector", "tensor", "gpsimd"]
D = 2048
DC = 16
NCORES = 8
TALL = 8192
TPC = 1024
SEQ = 2048
C = np.ascontiguousarray


class Prog:
    def __init__(self):
        self.items = []

    def add(self, eng, fn, inc=1):
        self.items.append((eng, fn, inc))

    def emit(self, nc, stack):
        sem = stack.enter_context(nc.semaphore("gsem"))
        block = stack.enter_context(nc.Block())
        scheds = {kk: [] for kk in ENGS}
        cum = 0
        for eng, fn, inc in self.items:
            scheds[eng].append((cum, fn, inc))
            cum += inc
        total = cum
        scheds["sync"].append((total, lambda e: e.wait_ge(sem, total), 0))
        for engname in ENGS:
            lst = scheds[engname]
            if not lst:
                continue

            def body(e, lst=lst):
                for before, fn, inc in lst:
                    if before > 0:
                        e.wait_ge(sem, before)
                    inst = fn(e)
                    if inc:
                        inst.then_inc(sem, inc)

            getattr(block, engname)(body)


class K:

    def __init__(self):
        self.nc = bass.Bass("TRN2", target_bir_lowering=False)
        self.stack = ExitStack()
        self.p = Prog()
        self._n = 0

    def _nm(self, pfx):
        self._n += 1
        return f"{pfx}{self._n}"

    def inp(self, name, shape, dt=F32):
        return self.nc.dram_tensor(name, list(shape), dt, kind="ExternalInput").ap()

    def outp(self, name, shape, dt=F32):
        return self.nc.dram_tensor(name, list(shape), dt, kind="ExternalOutput").ap()

    def sb(self, shape, dt=F32, stack=None):
        return (stack or self.stack).enter_context(
            self.nc.sbuf_tensor(self._nm("sb"), list(shape), dt))

    def ps(self, shape, dt=F32, stack=None):
        return (stack or self.stack).enter_context(
            self.nc.psum_tensor(self._nm("ps"), list(shape), dt))

    def dma(self, out, in_):
        self.p.add("sync", lambda e, o=out, i=in_: e.dma_start(out=o, in_=i), 16)

    def mm(self, out, lhsT, rhs, start=True, stop=True):
        self.p.add("tensor", lambda e, o=out, l=lhsT, r=rhs, s1=start, s2=stop:
                   e.matmul(o, l, r, start=s1, stop=s2))

    def tr(self, out, in_, eye):
        if not isinstance(eye, bass.AP):
            eye = eye[:]
        self.p.add("tensor", lambda e, o=out, i=in_, y=eye: e.transpose(o, i, y))

    def cp(self, out, in_):
        self.p.add("vector", lambda e, o=out, i=in_: e.tensor_copy(o, i))

    def act(self, out, in_, f, bias=0.0, scale=1.0):
        self.p.add("scalar", lambda e, o=out, i=in_, ff=f, b=bias, s=scale:
                   e.activation(o, i, ff, bias=b, scale=s))

    def tt(self, op, out, a, b):
        self.p.add("vector", lambda e, o=out, x=a, y=b, oo=op: e.tensor_tensor(o, x, y, oo))

    def ts(self, out, in0, s1, op0, s2=None, op1=None):
        if s2 is None:
            self.p.add("vector", lambda e, o=out, i=in0, a=s1, o0=op0:
                       e.tensor_single_scalar(o, i, a, o0))
        else:
            self.p.add("vector", lambda e, o=out, i=in0, a=s1, b=s2, o0=op0, o1=op1:
                       e.tensor_scalar(o, i, a, b, o0, o1))

    def stt(self, out, in0, scal, in1, op0, op1):
        self.p.add("vector", lambda e, o=out, i0=in0, s=scal, i1=in1, o0=op0, o1=op1:
                   e.scalar_tensor_tensor(o, i0, s, i1, o0, o1))

    def rsum(self, out, in_, negate=False):
        self.p.add("vector", lambda e, o=out, i=in_, n=negate:
                   e.reduce_sum(o, i, AX.X, negate=n))

    def rmax(self, out, in_, negate=False):
        self.p.add("vector", lambda e, o=out, i=in_, n=negate:
                   e.reduce_max(o, i, AX.X, negate=n))

    def recip(self, out, in_):
        self.p.add("vector", lambda e, o=out, i=in_: e.reciprocal(o, i))

    def scan(self, out, d0, d1, init, op0, op1):
        self.p.add("vector", lambda e, o=out, a=d0, b=d1, i=init, o0=op0, o1=op1:
                   e.tensor_tensor_scan(o, a, b, i, o0, o1))

    def memset(self, ap, v):
        self.p.add("vector", lambda e, a=ap, vv=v: e.memset(a, vv))

    def run(self, in_maps):
        self.p.emit(self.nc, self.stack)
        res = run_bass_kernel_spmd(self.nc, in_maps, core_ids=list(range(NCORES)))
        self.stack.close()
        return res.results

    def run_multi(self, in_maps_list):
        self.p.emit(self.nc, self.stack)
        out = []
        for im in in_maps_list:
            res = run_bass_kernel_spmd(self.nc, im, core_ids=list(range(NCORES)))
            out.append(res.results)
        self.stack.close()
        return out


def load_consts(k):
    eye = k.inp("eye", [128, 128])
    tri = k.inp("tri", [128, 128])
    ones = k.inp("ones1", [1, 128])
    sbEye = k.sb([128, 128])
    sbEyeB = k.sb([128, 128], BF16)
    sbTri = k.sb([128, 128])
    sbOnes = k.sb([1, 128])
    k.dma(sbEye[:], eye)
    k.dma(sbTri[:], tri)
    k.dma(sbOnes[:], ones)
    k.cp(sbEyeB[:], sbEye[:])
    return sbEye, sbEyeB, sbTri, sbOnes


def const_arrays():
    eye = np.eye(128, dtype=np.float32)
    tri = np.where(np.arange(128)[None, :] <= np.arange(128)[:, None], 0.0,
                   -1e30).astype(np.float32)
    ones = np.ones((1, 128), dtype=np.float32)
    return dict(eye=eye, tri=tri, ones1=ones)


def emit_ln_tiles(k, xdram, ntok, eyeB, xlnT, xpl_out=None):
    st = ExitStack()
    xt = k.sb([128, D], stack=st)
    xln = k.sb([128, D], stack=st)
    sq = k.sb([128, D], stack=st)
    xlb = k.sb([128, D], BF16, stack=st)
    st1 = k.sb([128, 1], stack=st)
    st2 = k.sb([128, 1], stack=st)
    psT = k.ps([128, 128], BF16, stack=st)
    for tc in range(ntok // 128):
        k.dma(xt[:], xdram[tc * 128:(tc + 1) * 128, :])
        k.rsum(st1[:], xt[:])
        k.ts(st1[:], st1[:], 1.0 / D, ALU.mult)
        k.ts(xln[:], xt[:], st1[:], ALU.subtract)
        k.act(sq[:], xln[:], AF.Square)
        k.rsum(st2[:], sq[:])
        k.ts(st2[:], st2[:], 1.0 / D, ALU.mult, 1e-5, ALU.add)
        k.act(st2[:], st2[:], AF.Sqrt)
        k.recip(st2[:], st2[:])
        k.ts(xln[:], xln[:], st2[:], ALU.mult)
        if xpl_out is not None:
            k.tt(ALU.add, sq[:], xln[:], xt[:])
            k.dma(xpl_out[tc * 128:(tc + 1) * 128, :], sq[:])
        k.cp(xlb[:], xln[:])
        for dc in range(DC):
            k.tr(psT[:], xlb[:, dc * 128:(dc + 1) * 128], eyeB)
            k.cp(xlnT[dc][:, tc * 128:(tc + 1) * 128], psT[:])
    st.close()


def bcast_row(k, row_ap, width, sbOnes, psB, dst):
    for j in range(0, width, 512):
        w = min(512, width - j)
        k.mm(psB[:, 0:w], sbOnes[:], row_ap[:, j:j + w])
        k.cp(dst[:, j:j + w], psB[:, 0:w])


def softmax_rows(k, psS, klen, Pb, nm, den, tri=None, diag0=None):
    if tri is not None:
        k.tt(ALU.add, psS[:, diag0:klen], psS[:, diag0:klen], tri[:])
    k.rmax(nm[:], psS[:, 0:klen], negate=True)
    k.act(Pb[:, 0:klen], psS[:, 0:klen], AF.Exp, bias=nm[:])
    k.rsum(den[:], Pb[:, 0:klen])
    k.recip(den[:], den[:])
    k.ts(Pb[:, 0:klen], Pb[:, 0:klen], den[:], ALU.mult)


def launch_att0(xb_list, wq_t, wk_t, wv_t, wf2, bf_vec, consts):
    HL, HD = 2, 64
    k = K()
    xb = k.inp("xb", [SEQ, D])
    wq = k.inp("wq", [D, 128])
    wk = k.inp("wk", [D, 128])
    wv = k.inp("wv", [D, 128])
    wf = k.inp("wf", [128, 32])
    bfc = k.inp("bfc", [1, 2])
    aAT = k.outp("aAT", [128, SEQ], BF16)

    sbEye, sbEyeB, sbTri, sbOnes = load_consts(k)
    psMain = k.ps([128, 2048])
    psPT = k.ps([128, 128], BF16)
    psO = k.ps([128, HD])
    psW = k.ps([128, 512])

    xlnT = [k.sb([128, SEQ], BF16) for _ in range(DC)]
    emit_ln_tiles(k, xb, SEQ, sbEyeB, xlnT)

    negFc = [k.sb([1, 2048]) for _ in range(2)]
    stF = ExitStack()
    wfsb = k.sb([128, 32], stack=stF)
    wfb = k.sb([128, 32], BF16, stack=stF)
    bfsb = k.sb([1, 2], stack=stF)
    zf = k.sb([1, 2048], stack=stF)
    zf2 = k.sb([1, 2048], stack=stF)
    k.dma(wfsb[:], wf)
    k.cp(wfb[:], wfsb[:])
    k.dma(bfsb[:], bfc)
    for l in range(2):
        for t4 in range(4):
            for dc in range(DC):
                k.mm(psW[0:1, :], wfb[:, dc * 2 + l:dc * 2 + l + 1],
                     xlnT[dc][:, t4 * 512:(t4 + 1) * 512],
                     start=(dc == 0), stop=(dc == DC - 1))
            k.ts(zf[:, t4 * 512:(t4 + 1) * 512], psW[0:1, :], bfsb[:, l:l + 1], ALU.add)
        k.act(zf2[:], zf[:], AF.Abs)
        k.act(zf2[:], zf2[:], AF.Exp, scale=-1.0)
        k.act(zf2[:], zf2[:], AF.Ln, bias=1.0)
        k.ts(negFc[l][:], zf[:], 0.0, ALU.min)
        k.tt(ALU.subtract, negFc[l][:], negFc[l][:], zf2[:])
        k.memset(zf2[:], 0.0)
        k.scan(zf[:], negFc[l][:], zf2[:], 0.0, ALU.add, ALU.add)
        k.ts(negFc[l][:], zf[:], -1.0, ALU.mult)
    stF.close()

    QT = k.sb([128, SEQ], BF16)
    KT = k.sb([128, SEQ], BF16)
    Vn = [k.sb([128, 128], BF16) for _ in range(16)]

    stW = ExitStack()
    wtmp = k.sb([128, 128], stack=stW)
    wbf = [k.sb([128, 128], BF16, stack=stW) for _ in range(DC)]
    for (wdram, OUT, scale) in ((wq, QT, 0.125), (wk, KT, 1.0)):
        for dc in range(DC):
            k.dma(wtmp[:], wdram[dc * 128:(dc + 1) * 128, :])
            k.cp(wbf[dc][:], wtmp[:])
        for t4 in range(4):
            for dc in range(DC):
                k.mm(psW[:], wbf[dc][:], xlnT[dc][:, t4 * 512:(t4 + 1) * 512],
                     start=(dc == 0), stop=(dc == DC - 1))
            if scale != 1.0:
                k.ts(OUT[:, t4 * 512:(t4 + 1) * 512], psW[:], scale, ALU.mult)
            else:
                k.cp(OUT[:, t4 * 512:(t4 + 1) * 512], psW[:])
    for dc in range(DC):
        k.dma(wtmp[:], wv[dc * 128:(dc + 1) * 128, :])
        k.cp(wbf[dc][:], wtmp[:])
    for tc in range(16):
        for dc in range(DC):
            k.mm(psW[:, 0:128], xlnT[dc][:, tc * 128:(tc + 1) * 128], wbf[dc][:],
                 start=(dc == 0), stop=(dc == DC - 1))
        k.cp(Vn[tc][:], psW[:, 0:128])
    stW.close()

    negB = k.sb([128, 2048])
    Pb = k.sb([128, 2048], BF16)
    nm = k.sb([128, 1])
    den = k.sb([128, 1])
    sbPT = k.sb([128, 128], BF16)
    sbO = k.sb([128, HD], BF16)
    aATl = k.sb([64, SEQ], BF16)
    psOT = k.ps([64, 128], BF16)

    for l in range(HL):
        bcast_row(k, negFc[l][:], 2048, sbOnes, psW, negB)
        po = l * 64
        for qi in range(16):
            klen = (qi + 1) * 128
            for j in range(0, klen, 512):
                w = min(512, klen - j)
                k.mm(psMain[:, j:j + w], QT[po:po + 64, qi * 128:(qi + 1) * 128],
                     KT[po:po + 64, j:j + w])
            k.tt(ALU.add, psMain[:, 0:klen], psMain[:, 0:klen], negB[:, 0:klen])
            softmax_rows(k, psMain, klen, Pb, nm, den, tri=sbTri, diag0=qi * 128)
            for kc in range(qi + 1):
                k.tr(psPT[:], Pb[:, kc * 128:(kc + 1) * 128], sbEyeB)
                k.cp(sbPT[:], psPT[:])
                k.mm(psO[:], sbPT[:], Vn[kc][:, l * 64:l * 64 + 64],
                     start=(kc == 0), stop=(kc == qi))
            k.cp(sbO[:], psO[:])
            k.tr(psOT[:, 0:128], sbO[:], sbEyeB)
            k.cp(aATl[:, qi * 128:(qi + 1) * 128], psOT[:, 0:128])
        k.dma(aAT[l * 64:(l + 1) * 64, :], aATl[:])

    maps = []
    for r in range(8):
        im = []
        for c in range(NCORES):
            b, hh = c // 2, c % 2
            hg = hh * 16 + r * 2
            im.append(dict(
                xb=xb_list[b],
                wq=C(wq_t[:, hg * 64:(hg + 2) * 64]),
                wk=C(wk_t[:, hg * 64:(hg + 2) * 64]),
                wv=C(wv_t[:, hg * 64:(hg + 2) * 64]),
                wf=C(wf2[:, :, hg:hg + 2]).reshape(128, 32),
                bfc=C(bf_vec[hg:hg + 2]).reshape(1, 2), **consts))
        maps.append(im)
    allres = k.run_multi(maps)
    out = np.empty((4, 2048, SEQ), dtype=BF)
    for r in range(8):
        for c in range(NCORES):
            b, hh = c // 2, c % 2
            hg = hh * 16 + r * 2
            out[b, hg * 64:(hg + 2) * 64, :] = allres[r][c]["aAT"]
    return out


def launch_proj(x_in, aT_b, wT, consts):
    k = K()
    xs = k.inp("xs", [TPC, D])
    aT = k.inp("aT", [D, TPC], BF16)
    w = k.inp("w", [D, D])
    y = k.outp("y", [TPC, D])
    sbEye, sbEyeB, sbTri, sbOnes = load_consts(k)
    psMain = k.ps([128, 2048])
    aTs = [k.sb([128, TPC], BF16) for _ in range(DC)]
    wsb = [k.sb([128, D], BF16) for _ in range(DC)]
    wtmp = k.sb([128, D])
    xt = k.sb([128, D])
    ys = k.sb([128, D])
    for dc in range(DC):
        k.dma(aTs[dc][:], aT[dc * 128:(dc + 1) * 128, :])
        k.dma(wtmp[:], w[dc * 128:(dc + 1) * 128, :])
        k.cp(wsb[dc][:], wtmp[:])
    for tc in range(TPC // 128):
        k.dma(xt[:], xs[tc * 128:(tc + 1) * 128, :])
        for os in range(4):
            for dc in range(DC):
                k.mm(psMain[:, os * 512:(os + 1) * 512],
                     aTs[dc][:, tc * 128:(tc + 1) * 128],
                     wsb[dc][:, os * 512:(os + 1) * 512],
                     start=(dc == 0), stop=(dc == DC - 1))
        k.tt(ALU.add, ys[:], psMain[:], xt[:])
        k.dma(y[tc * 128:(tc + 1) * 128, :], ys[:])
    in_maps = []
    for c in range(NCORES):
        b, h = c // 2, c % 2
        in_maps.append(dict(
            xs=C(x_in[c * TPC:(c + 1) * TPC, :]),
            aT=C(aT_b[b, :, h * 1024:(h + 1) * 1024]),
            w=wT, **consts))
    res = k.run(in_maps)
    return np.concatenate([res[c]["y"] for c in range(NCORES)], axis=0)


def launch_ffn0(x_in, wg_t, wu_t, wd_n, sgv, suv, sdv, consts):
    k = K()
    xs = k.inp("xs", [TPC, D])
    wg = k.inp("wg", [64, 128, 16, 128], I8)
    wu = k.inp("wu", [64, 128, 16, 128], I8)
    wd = k.inp("wd", [8192, D], I8)
    sg = k.inp("sg", [128, 64])
    su = k.inp("su", [128, 64])
    sd = k.inp("sd", [1, D])
    y = k.outp("y", [TPC, D])
    sbEye, sbEyeB, sbTri, sbOnes = load_consts(k)
    psG = k.ps([128, 1024])
    psU = k.ps([128, 1024])

    xlnT = [k.sb([128, TPC], BF16) for _ in range(DC)]
    emit_ln_tiles(k, xs, TPC, sbEyeB, xlnT)
    psD = k.ps([128, 2048])

    ysb = [k.sb([128, D]) for _ in range(8)]
    sgs = k.sb([128, 64])
    sus = k.sb([128, 64])
    sdB = k.sb([128, D])
    stS = ExitStack()
    sds = k.sb([1, D], stack=stS)
    k.dma(sgs[:], sg)
    k.dma(sus[:], su)
    k.dma(sds[:], sd)
    bcast_row(k, sds[:], D, sbOnes, psG[:, 0:512], sdB)
    stS.close()

    stW2 = ExitStack()
    w8 = k.sb([128, 2048], I8, stack=stW2)
    wgB = k.sb([128, 2048], BF16, stack=stW2)
    wuB = k.sb([128, 2048], BF16, stack=stW2)
    wdB = [k.sb([128, D], BF16, stack=stW2) for _ in range(8)]
    fsb = [k.sb([128, TPC], BF16, stack=stW2) for _ in range(8)]
    gg = k.sb([128, TPC], stack=stW2)
    uu = k.sb([128, TPC], stack=stW2)

    for g in range(8):
        for j in range(8):
            ic = g * 8 + j
            k.dma(w8[:], wg[ic].rearrange("p a b -> p (a b)"))
            k.cp(wgB[:], w8[:])
            k.dma(w8[:], wu[ic].rearrange("p a b -> p (a b)"))
            k.cp(wuB[:], w8[:])
            k.dma(w8[:], wd[ic * 128:(ic + 1) * 128, :])
            k.cp(wdB[j][:], w8[:])
            for h2 in range(2):
                for dc in range(DC):
                    k.mm(psG[:, h2 * 512:(h2 + 1) * 512], wgB[:, dc * 128:(dc + 1) * 128],
                         xlnT[dc][:, h2 * 512:(h2 + 1) * 512],
                         start=(dc == 0), stop=(dc == DC - 1))
                for dc in range(DC):
                    k.mm(psU[:, h2 * 512:(h2 + 1) * 512], wuB[:, dc * 128:(dc + 1) * 128],
                         xlnT[dc][:, h2 * 512:(h2 + 1) * 512],
                         start=(dc == 0), stop=(dc == DC - 1))
            k.act(gg[:], psG[:], AF.Silu, scale=sgs[:, ic:ic + 1])
            k.ts(uu[:], psU[:], sus[:, ic:ic + 1], ALU.mult)
            k.tt(ALU.mult, fsb[j][:], gg[:], uu[:])
        for tc in range(8):
            for j in range(8):
                for os in range(4):
                    k.mm(psD[:, os * 512:(os + 1) * 512],
                         fsb[j][:, tc * 128:(tc + 1) * 128],
                         wdB[j][:, os * 512:(os + 1) * 512],
                         start=(j == 0), stop=(j == 7))
            if g == 0:
                k.cp(ysb[tc][:], psD[:])
            else:
                k.tt(ALU.add, ysb[tc][:], ysb[tc][:], psD[:])
    stW2.close()
    xt2 = k.sb([128, D])
    for tc in range(8):
        k.dma(xt2[:], xs[tc * 128:(tc + 1) * 128, :])
        k.tt(ALU.mult, ysb[tc][:], ysb[tc][:], sdB[:])
        k.tt(ALU.add, ysb[tc][:], ysb[tc][:], xt2[:])
        k.dma(y[tc * 128:(tc + 1) * 128, :], ysb[tc][:])
    in_maps = [dict(xs=C(x_in[c * TPC:(c + 1) * TPC, :]), wg=wg_t, wu=wu_t, wd=wd_n,
                    sg=sgv, su=suv, sd=sdv, **consts) for c in range(NCORES)]
    res = k.run(in_maps)
    return np.concatenate([res[c]["y"] for c in range(NCORES)], axis=0)


def launch_att1(xb_list, wq_t, bq2, wk_t, bk2, wv_t, bv_row, consts):
    HL = 2
    k = K()
    xb = k.inp("xb", [SEQ, D])
    xr = k.inp("xr", [SEQ, 256])
    wq = k.inp("wq", [D, 256])
    wk = k.inp("wk", [D, 128])
    wv = k.inp("wv", [D, 128])
    bq = k.inp("bq", [128, 2])
    bk = k.inp("bk", [128, 1])
    bv = k.inp("bv", [1, 128])
    yp = k.outp("yp", [SEQ, 256])

    sbEye, sbEyeB, sbTri, sbOnes = load_consts(k)
    psMain = k.ps([128, 2048])
    psPT = k.ps([128, 128], BF16)
    psO = k.ps([128, 128])
    psW = k.ps([128, 512])

    xlnT = [k.sb([128, SEQ], BF16) for _ in range(DC)]
    emit_ln_tiles(k, xb, SEQ, sbEyeB, xlnT)

    QT = [k.sb([128, SEQ], BF16) for _ in range(HL)]
    KT = k.sb([128, SEQ], BF16)
    Vn = [k.sb([128, 128], BF16) for _ in range(16)]

    stW = ExitStack()
    wtmp = k.sb([128, 128], stack=stW)
    wbf = [k.sb([128, 128], BF16, stack=stW) for _ in range(DC)]
    bqs = k.sb([128, 2], stack=stW)
    bks = k.sb([128, 1], stack=stW)
    bvs = k.sb([1, 128], stack=stW)
    bvB = k.sb([128, 128], stack=stW)
    k.dma(bqs[:], bq)
    k.dma(bks[:], bk)
    k.dma(bvs[:], bv)
    bcast_row(k, bvs[:], 128, sbOnes, psW, bvB)

    SC = float(1.0 / np.sqrt(128.0))
    for oc in range(2):
        for dc in range(DC):
            k.dma(wtmp[:], wq[dc * 128:(dc + 1) * 128, oc * 128:(oc + 1) * 128])
            k.cp(wbf[dc][:], wtmp[:])
        for t4 in range(4):
            for dc in range(DC):
                k.mm(psW[:], wbf[dc][:], xlnT[dc][:, t4 * 512:(t4 + 1) * 512],
                     start=(dc == 0), stop=(dc == DC - 1))
            k.ts(QT[oc][:, t4 * 512:(t4 + 1) * 512], psW[:], bqs[:, oc:oc + 1],
                 ALU.add, SC, ALU.mult)
    for dc in range(DC):
        k.dma(wtmp[:], wk[dc * 128:(dc + 1) * 128, :])
        k.cp(wbf[dc][:], wtmp[:])
    for t4 in range(4):
        for dc in range(DC):
            k.mm(psW[:], wbf[dc][:], xlnT[dc][:, t4 * 512:(t4 + 1) * 512],
                 start=(dc == 0), stop=(dc == DC - 1))
        k.ts(KT[:, t4 * 512:(t4 + 1) * 512], psW[:], bks[:], ALU.add)
    for dc in range(DC):
        k.dma(wtmp[:], wv[dc * 128:(dc + 1) * 128, :])
        k.cp(wbf[dc][:], wtmp[:])
    for tc in range(16):
        for dc in range(DC):
            k.mm(psW[:, 0:128], xlnT[dc][:, tc * 128:(tc + 1) * 128], wbf[dc][:],
                 start=(dc == 0), stop=(dc == DC - 1))
        k.tt(ALU.add, Vn[tc][:], psW[:, 0:128], bvB[:])
    stW.close()

    Pb = k.sb([128, 2048], BF16)
    nm = k.sb([128, 1])
    den = k.sb([128, 1])
    sbPT = k.sb([128, 128], BF16)
    xres = k.sb([128, 256])
    ysb = k.sb([128, 256])
    for qi in range(16):
        klen = (qi + 1) * 128
        k.dma(xres[:], xr[qi * 128:(qi + 1) * 128, :])
        for l in range(HL):
            for j in range(0, klen, 512):
                w = min(512, klen - j)
                k.mm(psMain[:, j:j + w], QT[l][:, qi * 128:(qi + 1) * 128], KT[:, j:j + w])
            softmax_rows(k, psMain, klen, Pb, nm, den, tri=sbTri, diag0=qi * 128)
            for kc in range(qi + 1):
                k.tr(psPT[:], Pb[:, kc * 128:(kc + 1) * 128], sbEyeB)
                k.cp(sbPT[:], psPT[:])
                k.mm(psO[:], sbPT[:], Vn[kc][:],
                     start=(kc == 0), stop=(kc == qi))
            k.tt(ALU.add, ysb[:, l * 128:(l + 1) * 128], psO[:], xres[:, l * 128:(l + 1) * 128])
        k.dma(yp[qi * 128:(qi + 1) * 128, :], ysb[:])

    maps = []
    for r in range(4):
        im = []
        for c in range(NCORES):
            b, hh = c // 2, c % 2
            h0 = hh * 8 + r * 2
            g = h0 // 4
            im.append(dict(
                xb=xb_list[b], xr=C(xb_list[b][:, h0 * 128:(h0 + 2) * 128]),
                wq=C(wq_t[:, h0 * 128:(h0 + 2) * 128]),
                wk=C(wk_t[:, g * 128:(g + 1) * 128]),
                wv=C(wv_t[:, g * 128:(g + 1) * 128]),
                bq=C(bq2[:, h0:h0 + 2]),
                bk=C(bk2[:, g:g + 1]),
                bv=C(bv_row[:, g * 128:(g + 1) * 128]), **consts))
        maps.append(im)
    allres = k.run_multi(maps)
    out = np.empty((TALL, D), np.float32)
    for r in range(4):
        for c in range(NCORES):
            b, hh = c // 2, c % 2
            h0 = hh * 8 + r * 2
            out[b * SEQ:(b + 1) * SEQ, h0 * 128:(h0 + 2) * 128] = allres[r][c]["yp"]
    return out


def launch_rope(xb_list, wq_t, wg_t, wk_t, wv_t, norms, trig, consts):
    HL = 2
    k = K()
    xb = k.inp("xb", [SEQ, D])
    wq = k.inp("wq", [D, 256])
    wgt = k.inp("wgt", [D, 256])
    wk = k.inp("wk", [D, 128])
    wv = k.inp("wv", [D, 128])
    qnw = k.inp("qnw", [1, 128])
    qnb = k.inp("qnb", [1, 128])
    knw = k.inp("knw", [1, 128])
    knb = k.inp("knb", [1, 128])
    cosT = k.inp("cosT", [SEQ, 64])
    sinT = k.inp("sinT", [SEQ, 64])
    cosS = k.inp("cosS", [SEQ, 64])
    sinS = k.inp("sinS", [SEQ, 64])
    gAT = k.outp("gAT", [256, SEQ], BF16)

    sbEye, sbEyeB, sbTri, sbOnes = load_consts(k)
    psMain = k.ps([128, 2048])
    psPT = k.ps([128, 128], BF16)
    psO = k.ps([128, 128])
    psW = k.ps([128, 512])

    QT = [k.sb([128, SEQ], BF16) for _ in range(HL)]
    KT = k.sb([128, SEQ], BF16)
    Vn = [k.sb([128, 128], BF16) for _ in range(16)]
    gsigN = [k.sb([128, 256], BF16) for _ in range(16)]

    nw = k.sb([1, 128])
    qnwB = k.sb([128, 128])
    qnbB = k.sb([128, 128])
    knwB = k.sb([128, 128])
    knbB = k.sb([128, 128])
    for (dr, dst) in ((qnw, qnwB), (qnb, qnbB), (knw, knwB), (knb, knbB)):
        k.dma(nw[:], dr)
        bcast_row(k, nw[:], 128, sbOnes, psW, dst)

    xlnT = [k.sb([128, SEQ], BF16) for _ in range(DC)]
    emit_ln_tiles(k, xb, SEQ, sbEyeB, xlnT)

    stQ = ExitStack()
    wtmp = k.sb([128, 256], stack=stQ)
    wqb = [k.sb([128, 256], BF16, stack=stQ) for _ in range(DC)]
    wkb = [k.sb([128, 128], BF16, stack=stQ) for _ in range(DC)]
    wvb = [k.sb([128, 128], BF16, stack=stQ) for _ in range(DC)]
    wgb = [k.sb([128, 256], BF16, stack=stQ) for _ in range(DC)]
    qn_ = k.sb([128, 256], stack=stQ)
    qr_ = k.sb([128, 256], BF16, stack=stQ)
    kr_ = k.sb([128, 128], BF16, stack=stQ)
    ct = k.sb([128, 64], stack=stQ)
    st_ = k.sb([128, 64], stack=stQ)
    cs = k.sb([128, 64], stack=stQ)
    ss = k.sb([128, 64], stack=stQ)
    xc = k.sb([128, 128], stack=stQ)
    sq_ = k.sb([128, 128], stack=stQ)
    tE = k.sb([128, 64], stack=stQ)
    tO = k.sb([128, 64], stack=stQ)
    m1 = k.sb([128, 1], stack=stQ)
    m2 = k.sb([128, 1], stack=stQ)

    for dc in range(DC):
        k.dma(wtmp[:], wq[dc * 128:(dc + 1) * 128, :])
        k.cp(wqb[dc][:], wtmp[:])
        k.dma(wtmp[:, 0:128], wk[dc * 128:(dc + 1) * 128, :])
        k.cp(wkb[dc][:], wtmp[:, 0:128])
        k.dma(wtmp[:, 0:128], wv[dc * 128:(dc + 1) * 128, :])
        k.cp(wvb[dc][:], wtmp[:, 0:128])
        k.dma(wtmp[:], wgt[dc * 128:(dc + 1) * 128, :])
        k.cp(wgb[dc][:], wtmp[:])

    def head_ln(src_ap, wB, bB):
        k.rsum(m1[:], src_ap)
        k.ts(m1[:], m1[:], 1.0 / 128, ALU.mult)
        k.ts(xc[:], src_ap, m1[:], ALU.subtract)
        k.act(sq_[:], xc[:], AF.Square)
        k.rsum(m2[:], sq_[:])
        k.ts(m2[:], m2[:], 1.0 / 128, ALU.mult, 1e-6, ALU.add)
        k.act(m2[:], m2[:], AF.Sqrt)
        k.recip(m2[:], m2[:])
        k.ts(xc[:], xc[:], m2[:], ALU.mult)
        k.tt(ALU.mult, xc[:], xc[:], wB[:])
        k.tt(ALU.add, xc[:], xc[:], bB[:])

    def rope_into(dst_even, dst_odd, c_, s_):
        k.tt(ALU.mult, tE[:], xc[:, 0::2], c_[:])
        k.tt(ALU.mult, tO[:], xc[:, 1::2], s_[:])
        k.tt(ALU.subtract, tE[:], tE[:], tO[:])
        k.cp(dst_even, tE[:])
        k.tt(ALU.mult, tE[:], xc[:, 0::2], s_[:])
        k.tt(ALU.mult, tO[:], xc[:, 1::2], c_[:])
        k.tt(ALU.add, tE[:], tE[:], tO[:])
        k.cp(dst_odd, tE[:])

    for tc in range(16):
        k.dma(ct[:], cosT[tc * 128:(tc + 1) * 128, :])
        k.dma(st_[:], sinT[tc * 128:(tc + 1) * 128, :])
        k.dma(cs[:], cosS[tc * 128:(tc + 1) * 128, :])
        k.dma(ss[:], sinS[tc * 128:(tc + 1) * 128, :])
        for dc in range(DC):
            k.mm(psW[:, 0:256], xlnT[dc][:, tc * 128:(tc + 1) * 128], wqb[dc][:],
                 start=(dc == 0), stop=(dc == DC - 1))
        k.cp(qn_[:], psW[:, 0:256])
        for l in range(HL):
            head_ln(qn_[:, l * 128:(l + 1) * 128], qnwB, qnbB)
            rope_into(qr_[:, l * 128:(l + 1) * 128:2],
                      qr_[:, l * 128 + 1:(l + 1) * 128:2], cs, ss)
        for l in range(HL):
            k.tr(psPT[:], qr_[:, l * 128:(l + 1) * 128], sbEyeB)
            k.cp(QT[l][:, tc * 128:(tc + 1) * 128], psPT[:])
        for dc in range(DC):
            k.mm(psW[:, 0:128], xlnT[dc][:, tc * 128:(tc + 1) * 128], wkb[dc][:],
                 start=(dc == 0), stop=(dc == DC - 1))
        k.cp(qn_[:, 0:128], psW[:, 0:128])
        head_ln(qn_[:, 0:128], knwB, knbB)
        rope_into(kr_[:, 0:128:2], kr_[:, 1:128:2], ct, st_)
        k.tr(psPT[:], kr_[:], sbEyeB)
        k.cp(KT[:, tc * 128:(tc + 1) * 128], psPT[:])
        for dc in range(DC):
            k.mm(psW[:, 0:128], xlnT[dc][:, tc * 128:(tc + 1) * 128], wvb[dc][:],
                 start=(dc == 0), stop=(dc == DC - 1))
        k.cp(Vn[tc][:], psW[:, 0:128])
        for dc in range(DC):
            k.mm(psW[:, 0:256], xlnT[dc][:, tc * 128:(tc + 1) * 128], wgb[dc][:],
                 start=(dc == 0), stop=(dc == DC - 1))
        k.act(gsigN[tc][:], psW[:, 0:256], AF.Sigmoid)
    stQ.close()

    Pb = k.sb([128, 2048], BF16)
    nm_ = k.sb([128, 1])
    den = k.sb([128, 1])
    sbPT = k.sb([128, 128], BF16)
    sbO = k.sb([128, 128], BF16)
    gATl = k.sb([128, SEQ], BF16)
    for l in range(HL):
        for qi in range(16):
            klen = (qi + 1) * 128
            for j in range(0, klen, 512):
                w = min(512, klen - j)
                k.mm(psMain[:, j:j + w], QT[l][:, qi * 128:(qi + 1) * 128], KT[:, j:j + w])
            softmax_rows(k, psMain, klen, Pb, nm_, den, tri=sbTri, diag0=qi * 128)
            for kc in range(qi + 1):
                k.tr(psPT[:], Pb[:, kc * 128:(kc + 1) * 128], sbEyeB)
                k.cp(sbPT[:], psPT[:])
                k.mm(psO[:], sbPT[:], Vn[kc][:],
                     start=(kc == 0), stop=(kc == qi))
            k.tt(ALU.mult, sbO[:], psO[:], gsigN[qi][:, l * 128:(l + 1) * 128])
            k.tr(psPT[:], sbO[:], sbEyeB)
            k.cp(gATl[:, qi * 128:(qi + 1) * 128], psPT[:])
        k.dma(gAT[l * 128:(l + 1) * 128, :], gATl[:])

    qn_a, qn_b, kn_a, kn_b = norms
    maps = []
    for r in range(4):
        im = []
        for c in range(NCORES):
            b, hh = c // 2, c % 2
            h0 = hh * 8 + r * 2
            g = h0 // 4
            im.append(dict(
                xb=xb_list[b],
                wq=C(wq_t[:, h0 * 128:(h0 + 2) * 128]),
                wgt=C(wg_t[:, h0 * 128:(h0 + 2) * 128]),
                wk=C(wk_t[:, g * 128:(g + 1) * 128]),
                wv=C(wv_t[:, g * 128:(g + 1) * 128]),
                qnw=qn_a, qnb=qn_b, knw=kn_a, knb=kn_b,
                cosT=trig[0], sinT=trig[1], cosS=trig[2], sinS=trig[3], **consts))
        maps.append(im)
    allres = k.run_multi(maps)
    out = np.empty((4, 2048, SEQ), dtype=BF)
    for r in range(4):
        for c in range(NCORES):
            b, hh = c // 2, c % 2
            h0 = hh * 8 + r * 2
            out[b, h0 * 128:(h0 + 2) * 128, :] = allres[r][c]["gAT"]
    return out


def launch_moe(x_in, wr2, wsg2, Wsg_t, Wsu_t, Wsd_n, Weg_t, Weu_t, Wed_n, consts):
    k = K()
    xs = k.inp("xs", [TPC, D])
    wr = k.inp("wr", [128, 128])
    wsg = k.inp("wsg", [128, 16])
    comb = k.outp("comb", [TPC, 8])
    sgs = k.outp("sgs", [1, TPC])
    sbEye, sbEyeB, sbTri, sbOnes = load_consts(k)
    psG = k.ps([128, 1024])
    psT2 = k.ps([128, 128])
    xlnT = [k.sb([128, TPC], BF16) for _ in range(DC)]
    emit_ln_tiles(k, xs, TPC, sbEyeB, xlnT)
    wrs = k.sb([128, 128])
    wrb = k.sb([128, 128], BF16)
    logT = k.sb([8, 1024])
    probs = k.sb([128, 8])
    tmp8 = k.sb([128, 8])
    m1 = k.sb([128, 1])
    m2 = k.sb([128, 1])
    wsgs = k.sb([128, 16])
    wsgb = k.sb([128, 16], BF16)
    sgT = k.sb([1, 1024])
    k.dma(wrs[:], wr)
    k.cp(wrb[:], wrs[:])
    k.dma(wsgs[:], wsg)
    k.cp(wsgb[:], wsgs[:])
    for h2 in range(2):
        for dc in range(DC):
            k.mm(psG[0:8, 0:512], wrb[:, dc * 8:(dc + 1) * 8],
                 xlnT[dc][:, h2 * 512:(h2 + 1) * 512],
                 start=(dc == 0), stop=(dc == DC - 1))
        k.cp(logT[:, h2 * 512:(h2 + 1) * 512], psG[0:8, 0:512])
    for sl in range(8):
        k.tr(psT2[:, 0:8], logT[:, sl * 128:(sl + 1) * 128], sbEye[0:8, 0:8])
        k.cp(probs[:], psT2[:, 0:8])
        k.rmax(m1[:], probs[:], negate=True)
        k.act(probs[:], probs[:], AF.Exp, bias=m1[:])
        k.rsum(m2[:], probs[:])
        k.recip(m2[:], m2[:])
        k.ts(probs[:], probs[:], m2[:], ALU.mult)
        k.rmax(m1[:], probs[:])
        k.ts(tmp8[:], probs[:], m1[:], ALU.is_ge)
        k.stt(tmp8[:], tmp8[:], -1e30, probs[:], ALU.mult, ALU.add)
        k.rmax(m2[:], tmp8[:])
        k.ts(tmp8[:], probs[:], m2[:], ALU.is_ge)
        k.tt(ALU.mult, probs[:], probs[:], tmp8[:])
        k.dma(comb[sl * 128:(sl + 1) * 128, :], probs[:])
    for h2 in range(2):
        for dc in range(DC):
            k.mm(psG[0:1, 0:512], wsgb[:, dc:dc + 1], xlnT[dc][:, h2 * 512:(h2 + 1) * 512],
                 start=(dc == 0), stop=(dc == DC - 1))
        k.cp(sgT[:, h2 * 512:(h2 + 1) * 512], psG[0:1, 0:512])
    k.act(sgT[:], sgT[:], AF.Sigmoid)
    k.dma(sgs, sgT[:])
    rres = k.run([dict(xs=C(x_in[c * TPC:(c + 1) * TPC, :]), wr=wr2, wsg=wsg2, **consts)
                  for c in range(NCORES)])
    combF = np.concatenate([rres[c]["comb"] for c in range(NCORES)], axis=0)
    sgsF = np.concatenate([rres[c]["sgs"] for c in range(NCORES)], axis=1)

    k = K()
    xs = k.inp("xs", [TPC, D])
    acc = k.inp("acc", [TPC, D])
    wG = k.inp("wG", [11, 128, 16, 128])
    wU = k.inp("wU", [11, 128, 16, 128])
    wD = k.inp("wD", [1408, D])
    wrow = k.inp("wrow", [1, TPC])
    y = k.outp("y", [TPC, D])
    sbEye, sbEyeB, sbTri, sbOnes = load_consts(k)
    psG = k.ps([128, 1024])
    psU = k.ps([128, 1024])
    xlnT = [k.sb([128, TPC], BF16) for _ in range(DC)]
    emit_ln_tiles(k, xs, TPC, sbEyeB, xlnT)
    psD = k.ps([128, 2048])
    ysb = [k.sb([128, D]) for _ in range(8)]
    for tc in range(8):
        k.dma(ysb[tc][:], acc[tc * 128:(tc + 1) * 128, :])
    wrs = k.sb([1, TPC])
    wB = k.sb([128, TPC])
    k.dma(wrs[:], wrow)
    bcast_row(k, wrs[:], TPC, sbOnes, psG[:, 0:512], wB)
    wtmp = k.sb([128, 2048])
    wgB = k.sb([128, 2048], BF16)
    wuB = k.sb([128, 2048], BF16)
    wdB = [k.sb([128, D], BF16) for _ in range(8)]
    fsb = [k.sb([128, TPC], BF16) for _ in range(8)]
    gg = k.sb([128, TPC])
    uu = k.sb([128, TPC])
    for g, nj in ((0, 8), (1, 3)):
        for j in range(nj):
            ic = g * 8 + j
            k.dma(wtmp[:], wG[ic].rearrange("p a b -> p (a b)"))
            k.cp(wgB[:], wtmp[:])
            k.dma(wtmp[:], wU[ic].rearrange("p a b -> p (a b)"))
            k.cp(wuB[:], wtmp[:])
            k.dma(wtmp[:], wD[ic * 128:(ic + 1) * 128, :])
            k.cp(wdB[j][:], wtmp[:])
            for h2 in range(2):
                for dc in range(DC):
                    k.mm(psG[:, h2 * 512:(h2 + 1) * 512], wgB[:, dc * 128:(dc + 1) * 128],
                         xlnT[dc][:, h2 * 512:(h2 + 1) * 512],
                         start=(dc == 0), stop=(dc == DC - 1))
                for dc in range(DC):
                    k.mm(psU[:, h2 * 512:(h2 + 1) * 512], wuB[:, dc * 128:(dc + 1) * 128],
                         xlnT[dc][:, h2 * 512:(h2 + 1) * 512],
                         start=(dc == 0), stop=(dc == DC - 1))
            k.act(gg[:], psG[:], AF.Silu)
            k.tt(ALU.mult, uu[:], gg[:], psU[:])
            k.tt(ALU.mult, fsb[j][:], uu[:], wB[:])
        for tc in range(8):
            for j in range(nj):
                for os in range(4):
                    k.mm(psD[:, os * 512:(os + 1) * 512],
                         fsb[j][:, tc * 128:(tc + 1) * 128],
                         wdB[j][:, os * 512:(os + 1) * 512],
                         start=(j == 0), stop=(j == nj - 1))
            k.tt(ALU.add, ysb[tc][:], ysb[tc][:], psD[:])
    for tc in range(8):
        k.dma(y[tc * 128:(tc + 1) * 128, :], ysb[tc][:])
    k.p.emit(k.nc, k.stack)

    accv = x_in
    jobs = [("sh", ch) for ch in range(4)] + [("ex", e) for e in range(8)]
    for kind, idx in jobs:
        im = []
        for c in range(NCORES):
            if kind == "sh":
                wg_, wu_ = Wsg_t[idx], Wsu_t[idx]
                wd_ = C(Wsd_n[idx * 1408:(idx + 1) * 1408, :])
                wr_ = C(sgsF[:, c * TPC:(c + 1) * TPC])
            else:
                wg_, wu_ = Weg_t[idx], Weu_t[idx]
                wd_ = Wed_n[idx]
                wr_ = C(combF[c * TPC:(c + 1) * TPC, idx].reshape(1, TPC))
            im.append(dict(xs=C(x_in[c * TPC:(c + 1) * TPC, :]),
                           acc=C(accv[c * TPC:(c + 1) * TPC, :]),
                           wG=C(wg_), wU=C(wu_), wD=wd_, wrow=wr_, **consts))
        res = run_bass_kernel_spmd(k.nc, im, core_ids=list(range(NCORES))).results
        accv = np.concatenate([res[c]["y"] for c in range(NCORES)], axis=0)
    k.stack.close()
    return accv


def launch_rnna(x_in, wa2, ba_c, wb2, bb_c, consts):
    k = K()
    xs = k.inp("xs", [TPC, D])
    wa = k.inp("wa", [128, 1024])
    ba = k.inp("ba", [64, 1])
    wb = k.inp("wb", [128, 1024])
    bb = k.inp("bb", [64, 1])
    xpl = k.outp("xpl", [TPC, D])
    aT = k.outp("aT", [64, TPC])
    bT = k.outp("bT", [64, TPC])
    sbEye, sbEyeB, sbTri, sbOnes = load_consts(k)
    psW = k.ps([128, 512])
    xlnT = [k.sb([128, TPC], BF16) for _ in range(DC)]
    emit_ln_tiles(k, xs, TPC, sbEyeB, xlnT, xpl_out=xpl)
    was = k.sb([128, 1024])
    wab = k.sb([128, 1024], BF16)
    bas = k.sb([64, 1])
    ot = k.sb([64, TPC])
    for (wdr, bdr, odr) in ((wa, ba, aT), (wb, bb, bT)):
        k.dma(was[:], wdr)
        k.cp(wab[:], was[:])
        k.dma(bas[:], bdr)
        for h2 in range(2):
            for dc in range(DC):
                k.mm(psW[0:64, :], wab[:, dc * 64:(dc + 1) * 64],
                     xlnT[dc][:, h2 * 512:(h2 + 1) * 512],
                     start=(dc == 0), stop=(dc == DC - 1))
            k.ts(ot[:, h2 * 512:(h2 + 1) * 512], psW[0:64, :], bas[:], ALU.add)
        k.act(ot[:], ot[:], AF.Tanh)
        k.dma(odr, ot[:])
    in_maps = [dict(xs=C(x_in[c * TPC:(c + 1) * TPC, :]), wa=wa2, ba=ba_c,
                    wb=wb2, bb=bb_c, **consts) for c in range(NCORES)]
    res = k.run(in_maps)
    xplF = np.concatenate([res[c]["xpl"] for c in range(NCORES)], axis=0)
    aTF = np.concatenate([res[c]["aT"] for c in range(NCORES)], axis=1)
    bTF = np.concatenate([res[c]["bT"] for c in range(NCORES)], axis=1)
    return xplF, aTF, bTF


def launch_scan(aTF, bTF):
    k = K()
    a2 = k.inp("a2", [128, SEQ])
    b2 = k.inp("b2", [128, SEQ])
    hs = k.outp("hs", [128, SEQ])
    sA = k.sb([128, SEQ])
    sB = k.sb([128, SEQ])
    sH = k.sb([128, SEQ])
    k.dma(sA[:], a2)
    k.dma(sB[:], b2)
    k.scan(sH[:], sA[:], sB[:], 0.0, ALU.mult, ALU.add)
    k.dma(hs, sH[:])
    in_maps = []
    for c in range(NCORES):
        pb = 2 * (c % 2)
        a2 = np.concatenate([aTF[:, pb * SEQ:(pb + 1) * SEQ],
                             aTF[:, (pb + 1) * SEQ:(pb + 2) * SEQ]], axis=0)
        b2 = np.concatenate([bTF[:, pb * SEQ:(pb + 1) * SEQ],
                             bTF[:, (pb + 1) * SEQ:(pb + 2) * SEQ]], axis=0)
        in_maps.append(dict(a2=C(a2), b2=C(b2)))
    res = k.run(in_maps)
    hsT = np.empty((64, TALL), np.float32)
    for b in range(4):
        r = res[b // 2]["hs"]
        hsT[:, b * SEQ:(b + 1) * SEQ] = r[(b % 2) * 64:(b % 2) * 64 + 64, :]
    return hsT


def launch_rnnb(xpl, hsT, woutT, bout_r, consts):
    k = K()
    xs = k.inp("xs", [TPC, D])
    hT = k.inp("hT", [64, TPC])
    w = k.inp("w", [64, D])
    bo = k.inp("bo", [1, D])
    y = k.outp("y", [TPC, D])
    sbEye, sbEyeB, sbTri, sbOnes = load_consts(k)
    psD = k.ps([128, 2048])
    psW = k.ps([128, 512])
    hs = k.sb([64, TPC])
    hb = k.sb([64, TPC], BF16)
    ws = k.sb([64, D])
    wb_ = k.sb([64, D], BF16)
    bos = k.sb([1, D])
    boB = k.sb([128, D])
    xt = k.sb([128, D])
    ys = k.sb([128, D])
    k.dma(hs[:], hT)
    k.cp(hb[:], hs[:])
    k.dma(ws[:], w)
    k.cp(wb_[:], ws[:])
    k.dma(bos[:], bo)
    bcast_row(k, bos[:], D, sbOnes, psW, boB)
    for tc in range(8):
        k.dma(xt[:], xs[tc * 128:(tc + 1) * 128, :])
        for os in range(4):
            k.mm(psD[:, os * 512:(os + 1) * 512], hb[:, tc * 128:(tc + 1) * 128],
                 wb_[:, os * 512:(os + 1) * 512])
        k.tt(ALU.add, ys[:], psD[:], boB[:])
        k.tt(ALU.add, ys[:], ys[:], xt[:])
        k.dma(y[tc * 128:(tc + 1) * 128, :], ys[:])
    in_maps = [dict(xs=C(xpl[c * TPC:(c + 1) * TPC, :]),
                    hT=C(hsT[:, c * TPC:(c + 1) * TPC]),
                    w=woutT, bo=bout_r, **consts) for c in range(NCORES)]
    res = k.run(in_maps)
    return np.concatenate([res[c]["y"] for c in range(NCORES)], axis=0)


def kernel(**inp):
    f32 = np.float32
    consts = const_arrays()
    x = C(inp["L0_attn_a_hidden_states"].astype(f32, copy=False)).reshape(TALL, D)
    xb = lambda xx: [C(xx[b * SEQ:(b + 1) * SEQ, :]) for b in range(4)]

    wq_t = C(inp["L0_attn_a_Wq"].T)
    wk_t = C(inp["L0_attn_a_Wk"].T)
    wv_t = C(inp["L0_attn_a_Wv"].T)
    wf2 = C(inp["L0_attn_a_Wf"].T.reshape(16, 128, 32).transpose(1, 0, 2))
    aAT = launch_att0(xb(x), wq_t, wk_t, wv_t, wf2, inp["L0_attn_a_bf"].astype(f32), consts)
    x1 = launch_proj(x, aAT, C(inp["L0_attn_a_Wo"].T), consts)

    wg_t = C(inp["L0_ffn_w_gate"].reshape(16, 128, 64, 128).transpose(2, 1, 0, 3))
    wu_t = C(inp["L0_ffn_w_up"].reshape(16, 128, 64, 128).transpose(2, 1, 0, 3))
    wd_n = C(inp["L0_ffn_w_down"])
    sgv = C(inp["L0_ffn_s_gate"].reshape(64, 128).T)
    suv = C(inp["L0_ffn_s_up"].reshape(64, 128).T)
    sdv = C(inp["L0_ffn_s_down"].reshape(1, D))
    x2 = launch_ffn0(x1, wg_t, wu_t, wd_n, sgv, suv, sdv, consts)

    bwq_t = C(inp["L0_attn_b_Wq"].transpose(1, 0, 2).reshape(D, 2048))
    bwk_t = C(inp["L0_attn_b_Wk"].transpose(1, 0, 2).reshape(D, 512))
    bwv_t = C(inp["L0_attn_b_Wv"].transpose(1, 0, 2).reshape(D, 512))
    bq2 = C(inp["L0_attn_b_bq"].astype(f32).T)
    bk2 = C(inp["L0_attn_b_bk"].astype(f32).T)
    bv_row = C(inp["L0_attn_b_bv"].astype(f32).reshape(1, 512))
    x3 = launch_att1(xb(x2), bwq_t, bq2, bwk_t, bk2, bwv_t, bv_row, consts)

    x4 = launch_ffn0(x3, wg_t, wu_t, wd_n, sgv, suv, sdv, consts)

    wqkvT = C(inp["L1_attn_a_w_qkv"].T)
    SC = f32(1.0 / np.sqrt(128.0))
    pos = np.arange(SEQ, dtype=np.float64)
    inv = 1.0 / (1e6 ** (np.arange(0, 128, 2, dtype=np.float64) / 128.0))
    fr = np.outer(pos, inv)
    trig = (C(np.cos(fr).astype(f32)), C(np.sin(fr).astype(f32)),
            C((np.cos(fr) * SC).astype(f32)), C((np.sin(fr) * SC).astype(f32)))
    norms = (C(inp["L1_attn_a_q_norm_w"].astype(f32).reshape(1, 128)),
             C(inp["L1_attn_a_q_norm_b"].astype(f32).reshape(1, 128)),
             C(inp["L1_attn_a_k_norm_w"].astype(f32).reshape(1, 128)),
             C(inp["L1_attn_a_k_norm_b"].astype(f32).reshape(1, 128)))
    gAT = launch_rope(xb(x4), C(wqkvT[:, 0:2048]), C(wqkvT[:, 2048:4096]),
                      C(wqkvT[:, 4096:4608]), C(wqkvT[:, 4608:5120]), norms, trig, consts)
    x5 = launch_proj(x4, gAT, C(inp["L1_attn_a_w_o"].T), consts)

    wr2 = C(inp["L1_ffn_W_router"].T.reshape(16, 128, 8).transpose(1, 0, 2).reshape(128, 128))
    wsg2 = C(inp["L1_ffn_W_sgate"].T.reshape(16, 128, 1).transpose(1, 0, 2).reshape(128, 16))
    Wsgu = inp["L1_ffn_Ws_gate_up"]
    Wsg_t = C(Wsgu[:, :5632].reshape(16, 128, 4, 11, 128).transpose(2, 3, 1, 0, 4))
    Wsu_t = C(Wsgu[:, 5632:].reshape(16, 128, 4, 11, 128).transpose(2, 3, 1, 0, 4))
    Wsd_n = C(inp["L1_ffn_Ws_down"])
    Wegu = inp["L1_ffn_We_gate_up"]
    Weg_t = C(Wegu[:, :, :1408].reshape(8, 16, 128, 11, 128).transpose(0, 3, 2, 1, 4))
    Weu_t = C(Wegu[:, :, 1408:].reshape(8, 16, 128, 11, 128).transpose(0, 3, 2, 1, 4))
    Wed_n = C(inp["L1_ffn_We_down"])
    x6 = launch_moe(x5, wr2, wsg2, Wsg_t, Wsu_t, Wsd_n, Weg_t, Weu_t, Wed_n, consts)

    wa2 = C(inp["L1_attn_b_Wa"].T.reshape(16, 128, 64).transpose(1, 0, 2).reshape(128, 1024))
    wb2 = C(inp["L1_attn_b_Wb"].T.reshape(16, 128, 64).transpose(1, 0, 2).reshape(128, 1024))
    ba_c = C(inp["L1_attn_b_ba"].astype(f32).reshape(64, 1))
    bb_c = C(inp["L1_attn_b_bb"].astype(f32).reshape(64, 1))
    xpl, aTF, bTF = launch_rnna(x6, wa2, ba_c, wb2, bb_c, consts)
    hsT = launch_scan(aTF, bTF)
    x7 = launch_rnnb(xpl, hsT, C(inp["L1_attn_b_Wout"].T), 
                     C(inp["L1_attn_b_bout"].astype(f32).reshape(1, D)), consts)

    x8 = launch_moe(x7, wr2, wsg2, Wsg_t, Wsu_t, Wsd_n, Weg_t, Weu_t, Wed_n, consts)
    return x8.reshape(4, SEQ, D).astype(np.float32, copy=False)
